# version 02
# baseline (speedup 1.0000x reference)
import numpy as np
import ml_dtypes
import concourse.bass as bass
import concourse.mybir as mybir
from concourse.bass_utils import run_bass_kernel_spmd

F32 = mybir.dt.float32
BF16 = mybir.dt.bfloat16
U8 = mybir.dt.uint8
ALU = mybir.AluOpType
ACT = mybir.ActivationFunctionType
AX = mybir.AxisListType.X

NCORES = 8
T = 16384
D = 1024
TC = T // NCORES
NT = TC // 128
FCN = 8
E = 8
H = 4096
HCN = 32
NTB = 4
TBS = 512
EPS = 1e-5
DSCALE = 1.0 / 32.0
NEGBIG = -1e30

OFF_XNT = 0
OFF_EKV = 32768
OFF_XN16 = 98304
OFF_R1 = 131072
OFF_R2 = 163840
OFF_POOL = 196608
OFF_SMALL = 208896
ARENA_B = 212832

P_X2A = OFF_POOL + 0
P_X2B = OFF_POOL + 4096
P_STG = OFF_POOL + 8192
P_EQ0 = OFF_POOL + 0
P_EQ1 = OFF_POOL + 1024
P_GWT = OFF_POOL + 2048
P_ZST = OFF_POOL + 2304
P_GL = OFF_POOL + 4352
P_T8A = OFF_POOL + 4384
P_T8B = OFF_POOL + 4416
P_C16S = OFF_POOL + 4448
P_CST = OFF_POOL + 0
P_EQB = OFF_POOL + 4096
P_W2S = [OFF_POOL + 0, OFF_POOL + 2048]
P_B216 = OFF_POOL + 4096
P_XA = OFF_POOL + 6144
P_OST = OFF_POOL + 0

S_IDENT = OFF_SMALL + 0
S_ONES = OFF_SMALL + 256
S_B1ALL = OFF_SMALL + 288
S_COMB = OFF_SMALL + 1312
S_ZCOL = OFF_SMALL + 1824
S_SCOL = OFF_SMALL + 1888
S_COLS = OFF_SMALL + 1952
S_SST = OFF_SMALL + 2016
C_SUM, C_SQ, C_MEAN, C_VAR, C_RSTD, C_NMR, C_M1, C_M2, C_NEGM1, C_E1, \
    C_RDEN, C_D21, C_SC, C_RS, C_T0, C_T1 = range(16)

SEM_NAMES = ["g16", "s16", "c16", "tmm", "vev", "sev", "cc"]


def build_nc(dbg=False):
    nc = bass.Bass("TRN2", target_bir_lowering=False, debug=False)

    xg = nc.dram_tensor("x", [TC, D], F32, kind="ExternalInput").ap()
    wq = nc.dram_tensor("wq", [D, D], F32, kind="ExternalInput").ap()
    wk = nc.dram_tensor("wk", [D, D], F32, kind="ExternalInput").ap()
    wv = nc.dram_tensor("wv", [D, D], F32, kind="ExternalInput").ap()
    wo = nc.dram_tensor("wo", [D, D], F32, kind="ExternalInput").ap()
    bo = nc.dram_tensor("bo", [D], F32, kind="ExternalInput").ap()
    gam = nc.dram_tensor("gamma", [D], F32, kind="ExternalInput").ap()
    bet = nc.dram_tensor("beta", [D], F32, kind="ExternalInput").ap()
    gw = nc.dram_tensor("gate_w", [E, D], F32, kind="ExternalInput").ap()
    w1 = nc.dram_tensor("w1", [E, H, D], F32, kind="ExternalInput").ap()
    b1 = nc.dram_tensor("b1", [E, H], F32, kind="ExternalInput").ap()
    w2 = nc.dram_tensor("w2", [E, D, H], F32, kind="ExternalInput").ap()
    b2 = nc.dram_tensor("b2", [E, D], F32, kind="ExternalInput").ap()
    yg = nc.dram_tensor("y", [TC, D], F32, kind="ExternalOutput").ap()

    wqb = nc.dram_tensor("wqb", [D, D], BF16).ap()
    wkb = nc.dram_tensor("wkb", [D, D], BF16).ap()
    wvb = nc.dram_tensor("wvb", [D, D], BF16).ap()
    wob = nc.dram_tensor("wob", [D, D], BF16).ap()
    gwb = nc.dram_tensor("gwb", [E, D], BF16).ap()
    b1c = nc.dram_tensor("b1c", [E, H], BF16).ap()
    w1b = nc.dram_tensor("w1b", [E, H, D], BF16).ap()
    w2b = nc.dram_tensor("w2b", [E, D, H], BF16).ap()
    w1r = w1.rearrange("e h d -> (e h) d")
    w2r = w2.rearrange("e d (a b) -> (e d a) b", b=2048)
    w1br = w1b.rearrange("e h d -> (e h) d")
    w2br = w2b.rearrange("e d h -> (e d) h")
    eqd = nc.dram_tensor("eqd", [FCN, 128, TC], BF16).ap()
    eqdr = eqd.rearrange("f p t -> (f p) t")
    arin = nc.dram_tensor("arin", [1025, 1024], F32).ap()
    arout = nc.dram_tensor("arout", [1025, 1024], F32).ap()
    combd = nc.dram_tensor("combd", [TC, 128], BF16).ap()
    zrowd = nc.dram_tensor("zrowd", [TC], F32).ap()

    ident_d = nc.inline_tensor(np.eye(128, dtype=ml_dtypes.bfloat16),
                               name="identc").ap()
    dmp = {}
    if dbg:
        dmp["xn16"] = nc.dram_tensor("d_xn16", [128, 16384], BF16, kind="ExternalOutput").ap()
        dmp["ek"] = nc.dram_tensor("d_ek", [128, 16384], BF16, kind="ExternalOutput").ap()
        dmp["v"] = nc.dram_tensor("d_v", [128, 16384], BF16, kind="ExternalOutput").ap()
        dmp["arin"] = nc.dram_tensor("d_arin", [1025, 1024], F32, kind="ExternalOutput").ap()
        dmp["arout"] = nc.dram_tensor("d_arout", [1025, 1024], F32, kind="ExternalOutput").ap()
        dmp["eqd"] = nc.dram_tensor("d_eqd", [FCN * 128, TC], BF16, kind="ExternalOutput").ap()
        dmp["zrow"] = nc.dram_tensor("d_zrow", [1, TC], F32, kind="ExternalOutput").ap()
        dmp["comb"] = nc.dram_tensor("d_comb", [128, 128], F32, kind="ExternalOutput").ap()
        dmp["scol"] = nc.dram_tensor("d_scol", [128, 16], F32, kind="ExternalOutput").ap()
        dmp["c16"] = nc.dram_tensor("d_c16", [128, 8192], BF16, kind="ExternalOutput").ap()
        dmp["acc1"] = nc.dram_tensor("d_acc1", [128, 16384], F32, kind="ExternalOutput").ap()
        dmp["acc2"] = nc.dram_tensor("d_acc2", [128, 16384], F32, kind="ExternalOutput").ap()
        for _e in range(E):
            dmp[f"acce{_e}"] = nc.dram_tensor(f"d_acce{_e}", [128, 16384], F32, kind="ExternalOutput").ap()
        dmp["h0"] = nc.dram_tensor("d_h0", [128, 16384], BF16, kind="ExternalOutput").ap()

    arena = nc.alloc_sbuf_tensor("arena", [128, ARENA_B], U8)
    a16 = arena.ap().bitcast(BF16)
    a32 = arena.ap().bitcast(F32)

    def v16(off_b, cols, parts=128):
        return a16[0:parts, off_b // 2: off_b // 2 + cols]

    def v32(off_b, cols, parts=128):
        return a32[0:parts, off_b // 4: off_b // 4 + cols]

    def col(i):
        return v32(S_COLS + 4 * i, 1)

    pbh = [nc.alloc_psum_tensor(f"pb{i}", [128, 512], F32) for i in range(8)]
    pb = [t.ap() for t in pbh]
    pb16 = [t.ap().bitcast(BF16) for t in pbh]

    sems = {nm: nc.semaphore(nm).__enter__() for nm in SEM_NAMES}

    ident = v16(S_IDENT, 128)
    ones16 = v16(S_ONES, 1)
    b1all = v16(S_B1ALL, 256)
    comb = v32(S_COMB, 128)
    zcol = v32(S_ZCOL, 16)
    scol = v32(S_SCOL, 16)
    sstg = v32(S_SST, 8)
    borep = v32(OFF_R2 + 16384, 1024)
    gamrep = v32(OFF_R2 + 20480, 1024)
    betrep = v32(OFF_R2 + 24576, 1024)
    sqscr2 = v32(OFF_R2 + 28672, 1024)
    combT = v16(OFF_R2, 2048, parts=128)
    x2 = [v32(P_X2A, 1024), v32(P_X2B, 1024)]
    stg = v32(P_STG, 1024)
    eqstg = [v16(P_EQ0, 512), v16(P_EQ1, 512)]
    gwt16 = v16(P_GWT, 64)
    zstg = v32(P_ZST, 512, parts=1)
    gl = v32(P_GL, 8)
    t8a = v32(P_T8A, 8)
    t8b = v32(P_T8B, 8)
    c16stg = v16(P_C16S, 8)
    cstage = v32(P_CST, 1024)
    ztile = v32(P_CST, 1024)
    b216 = v16(P_B216, 1024, parts=8)
    xafin = v32(P_XA, 1024)
    ostg = v32(P_OST, 1024)
    gwsb = v16(P_X2A, 1024, parts=8)

    def emit(me, eng):
        n = {k: 0 for k in SEM_NAMES}
        lastw = {}

        def W(sem, val):
            if val <= 0 or lastw.get(sem, -1) >= val:
                return
            lastw[sem] = val
            if me is not None:
                eng.wait_ge(sems[sem], val)

        def dma(q, out, in_, **kw):
            ch = {"sync": "s16", "gpsimd": "g16", "scalar": "c16"}[q]
            n[ch] += 16
            if me == q:
                eng.dma_start(out=out, in_=in_, **kw).then_inc(sems[ch], 16)
            return n[ch]

        def dmat(q, out, in_):
            ch = {"sync": "s16", "gpsimd": "g16", "scalar": "c16"}[q]
            n[ch] += 16
            if me == q:
                eng.dma_start_transpose(out, in_).then_inc(sems[ch], 16)
            return n[ch]

        def op(engname, ch, f, drain=True):
            n[ch] += 1
            if me == engname:
                ins = f(eng)
                if drain and engname != "tensor":
                    eng.drain().then_inc(sems[ch], 1)
                else:
                    ins.then_inc(sems[ch], 1)
            return n[ch]

        def dr(engname):
            if me == engname:
                eng.drain()

        def opq(engname, f):
            if me == engname:
                f(eng)

        def mm_group(bank_ap, pairs, last_ck=True):
            m = len(pairs)
            for i, (lh, rh) in enumerate(pairs):
                st, sp = (i == 0), (i == m - 1)
                if i == m - 1 and last_ck:
                    ck = op("tensor", "tmm", lambda t, lh=lh, rh=rh, st=st, sp=sp:
                            t.matmul(bank_ap, lh, rh, start=st, stop=sp))
                else:
                    opq("tensor", lambda t, lh=lh, rh=rh, st=st, sp=sp:
                        t.matmul(bank_ap, lh, rh, start=st, stop=sp))
            return ck if last_ck else 0

        ck_ident = dma("gpsimd", ident, ident_d)
        dma("gpsimd", borep, bo.partition_broadcast(128))
        dma("gpsimd", gamrep, gam.partition_broadcast(128))
        ck_rep = dma("gpsimd", betrep, bet.partition_broadcast(128))
        ck_b1c = dma("gpsimd", b1c.rearrange("e (a b) -> (e a) b", b=2048),
                     b1.rearrange("e (a b) -> (e a) b", b=2048))
        W("g16", ck_b1c)
        ck_b1 = dmat("sync", b1all, b1c.flatten().rearrange("(r c) -> r c", c=128))
        ck_gwb = dma("gpsimd", gwb, gw)
        ck_wqb = dma("gpsimd", wqb, wq)
        ck_wkb = dma("gpsimd", wkb, wk)
        ck_wvb = dma("gpsimd", wvb, wv)
        ck_wob = dma("gpsimd", wob, wo)
        ck_ones = op("vector", "vev", lambda v: v.memset(ones16, 1.0))

        ck_xload = [0] * NT
        ck_xn = [0] * NT
        ck_sq = [0] * NT
        ck_varr = [0] * NT
        ck_xnT = [[0] * FCN for _ in range(NT)]
        for tt in range(NT):
            b = tt & 1
            if tt >= 2:
                W("vev", ck_xn[tt - 2])
                W("sev", ck_sq[tt - 2])
            ck_xload[tt] = dma("sync", x2[b], xg[tt * 128:(tt + 1) * 128, :])
            W("s16", ck_xload[tt])
            if tt >= 1:
                W("vev", ck_varr[tt - 1])
            ck_sq[tt] = op("scalar", "sev", lambda s, b=b: s.activation(
                stg, x2[b], ACT.Square))
            opq("vector", lambda v, b=b: v.reduce_sum(col(C_SUM), x2[b], axis=AX))
            W("sev", ck_sq[tt])
            opq("vector", lambda v: v.reduce_sum(col(C_SQ), stg, axis=AX))
            dr("vector")
            opq("vector", lambda v: v.tensor_scalar_mul(col(C_MEAN), col(C_SUM), 1.0 / D))
            opq("vector", lambda v: v.tensor_scalar(col(C_T0), col(C_SQ), 1.0 / D,
                                                    EPS, ALU.mult, ALU.add))
            dr("vector")
            opq("vector", lambda v: v.tensor_tensor(col(C_T1), col(C_MEAN), col(C_MEAN), ALU.mult))
            dr("vector")
            ck_varr[tt] = op("vector", "vev", lambda v: v.tensor_sub(
                col(C_VAR), col(C_T0), col(C_T1)))
            W("vev", ck_varr[tt])
            opq("scalar", lambda s: s.activation(col(C_T0), col(C_VAR), ACT.Ln))
            dr("scalar")
            ck_rstd = op("scalar", "sev", lambda s: s.activation(
                col(C_RSTD), col(C_T0), ACT.Exp, scale=-0.5))
            W("sev", ck_rstd)
            opq("vector", lambda v: v.scalar_tensor_tensor(
                col(C_NMR), col(C_MEAN), -1.0, col(C_RSTD), ALU.mult, ALU.mult))
            dr("vector")
            ck_xn[tt] = op("vector", "vev", lambda v, b=b, tt=tt: v.tensor_scalar(
                v16(OFF_XN16 + tt * 2048, 1024), x2[b], col(C_RSTD), col(C_NMR),
                ALU.mult, ALU.add))
            W("g16", ck_ident)
            W("vev", ck_xn[tt])
            for fc in range(FCN):
                pbank = 6 + (fc & 1)
                if fc >= 2:
                    W("sev", ck_xnT[tt][fc - 2])
                elif tt >= 1:
                    W("sev", ck_xnT[tt - 1][FCN - 2 + fc])
                tck = op("tensor", "tmm", lambda t, tt=tt, fc=fc, pbank=pbank:
                         t.transpose(pb16[pbank][:, 0:128],
                                     v16(OFF_XN16 + tt * 2048 + fc * 256, 128),
                                     ident))
                W("tmm", tck)
                ck_xnT[tt][fc] = op("scalar", "sev", lambda s, tt=tt, fc=fc, pbank=pbank:
                                    s.copy(v16(OFF_XNT + fc * 4096 + tt * 256, 128),
                                           pb16[pbank][:, 0:128]))

        if dbg:
            W("vev", ck_xn[NT - 1])
            dma("sync", dmp["xn16"], v16(OFF_XN16, 16384))
        W("g16", ck_wkb)
        ck_wkl = 0
        for fc in range(FCN):
            ck_wkl = dma("sync", v16(OFF_R1 + fc * 2048, 1024),
                         wkb[fc * 128:(fc + 1) * 128, :])
        W("g16", ck_wvb)
        ck_wvl = 0
        for fc in range(FCN):
            ck_wvl = dma("sync", v16(OFF_R1 + 16384 + fc * 2048, 1024),
                         wvb[fc * 128:(fc + 1) * 128, :])

        ck_kev = [[0, 0] for _ in range(NT)]
        ck_vv = [[0, 0] for _ in range(NT)]
        ck_kmm = [[0, 0] for _ in range(NT)]
        ck_vmm = [[0, 0] for _ in range(NT)]
        for tt in range(NT):
            for fc in range(FCN):
                W("sev", ck_xnT[tt][fc])
            for dh in range(2):
                if tt >= 1:
                    W("sev", ck_kev[tt - 1][dh])
                W("s16", ck_wkl)
                ck_kmm[tt][dh] = mm_group(
                    pb[dh][:, 0:512],
                    [(v16(OFF_XNT + fc * 4096 + tt * 256, 128),
                      v16(OFF_R1 + fc * 2048 + dh * 1024, 512))
                     for fc in range(FCN)])
                W("tmm", ck_kmm[tt][dh])
                ck_kev[tt][dh] = op("scalar", "sev", lambda s, tt=tt, dh=dh:
                                    s.activation(v16(OFF_EKV + tt * 2048 + dh * 1024, 512),
                                                 pb[dh][:, 0:512], ACT.Exp))
                if tt >= 1:
                    W("vev", ck_vv[tt - 1][dh])
                W("s16", ck_wvl)
                ck_vmm[tt][dh] = mm_group(
                    pb[2 + dh][:, 0:512],
                    [(v16(OFF_XNT + fc * 4096 + tt * 256, 128),
                      v16(OFF_R1 + 16384 + fc * 2048 + dh * 1024, 512))
                     for fc in range(FCN)])
                W("tmm", ck_vmm[tt][dh])
                ck_vv[tt][dh] = op("vector", "vev", lambda v, tt=tt, dh=dh:
                                   v.tensor_copy(v16(OFF_EKV + 32768 + tt * 2048 + dh * 1024, 512),
                                                 pb[2 + dh][:, 0:512]))
        ck_ek_all = ck_kev[NT - 1][1]
        ck_v_all = ck_vv[NT - 1][1]

        ck_cmm = [0] * FCN
        ck_cev = [0] * FCN
        ck_car = [0] * FCN
        for dc in range(FCN):
            p = dc & 1
            bank0, bank1 = 4 + 2 * p, 5 + 2 * p
            W("sev", ck_ek_all)
            W("vev", ck_v_all)
            if dc >= 2:
                W("vev", ck_cev[dc - 2])
            for tt in range(NT):
                st, sp = (tt == 0), (tt == NT - 1)
                lhs = v16(OFF_EKV + tt * 2048 + dc * 256, 128)
                opq("tensor", lambda t, lhs=lhs, tt=tt, st=st, sp=sp, bank0=bank0:
                    t.matmul(pb[bank0][:, 0:512], lhs,
                             v16(OFF_EKV + 32768 + tt * 2048, 512), start=st, stop=sp))
                opq("tensor", lambda t, lhs=lhs, tt=tt, st=st, sp=sp, bank1=bank1:
                    t.matmul(pb[bank1][:, 0:512], lhs,
                             v16(OFF_EKV + 32768 + tt * 2048 + 1024, 512), start=st, stop=sp))
                if tt == NT - 1:
                    ck_cmm[dc] = op("tensor", "tmm", lambda t, lhs=lhs, dc=dc, st=st, sp=sp:
                                    t.matmul(pb[0][:, dc:dc + 1], lhs, ones16,
                                             start=st, stop=sp))
                else:
                    opq("tensor", lambda t, lhs=lhs, dc=dc, st=st, sp=sp:
                        t.matmul(pb[0][:, dc:dc + 1], lhs, ones16, start=st, stop=sp))
            W("tmm", ck_cmm[dc])
            if dc >= 2:
                W("s16", ck_car[dc - 2])
            opq("vector", lambda v, p=p, bank0=bank0: v.tensor_copy(
                x2[p][:, 0:512], pb[bank0][:, 0:512]))
            ck_cev[dc] = op("vector", "vev", lambda v, p=p, bank1=bank1: v.tensor_copy(
                x2[p][:, 512:1024], pb[bank1][:, 0:512]))
            W("vev", ck_cev[dc])
            ck_car[dc] = dma("sync", arin[dc * 128:(dc + 1) * 128, :], x2[p])
        W("tmm", ck_cmm[FCN - 1])
        ck_sevS = op("vector", "vev", lambda v: v.tensor_copy(sstg, pb[0][:, 0:8]))
        W("vev", ck_sevS)
        ck_sar = dma("sync",
                     arin[1024:1025, :].flatten().rearrange("(c p) -> p c", p=128),
                     sstg, allow_slow_non_contiguous=True)

        if dbg:
            W("sev", ck_ek_all)
            W("vev", ck_v_all)
            dma("sync", dmp["ek"], v16(OFF_EKV, 16384))
            dma("sync", dmp["v"], v16(OFF_EKV + 32768, 16384))
            W("s16", ck_sar)
            dma("sync", dmp["arin"], arin)
        W("s16", ck_sar)
        n["cc"] += 1
        if me == "gpsimd":
            eng.collective_compute(
                "AllReduce", ALU.add, replica_groups=[list(range(NCORES))],
                ins=[arin.opt()], outs=[arout.opt()]).then_inc(sems["cc"], 1)
        ck_cc = n["cc"]

        ck_w1c = [0] * E
        ck_w2c = [0] * E
        for e in range(E):
            ck_w1c[e] = dma("gpsimd", w1br[e * H:(e + 1) * H, :],
                            w1r[e * H:(e + 1) * H, :])
            ck_w2c[e] = dma("gpsimd",
                            w2br[e * D:(e + 1) * D, :].rearrange(
                                "d (a b) -> (d a) b", b=2048),
                            w2r[e * D * 2:(e + 1) * D * 2, :])

        W("g16", ck_wqb)
        W("tmm", ck_kmm[NT - 1][1])
        ck_wql = 0
        for fc in range(FCN):
            ck_wql = dma("sync", v16(OFF_R1 + fc * 2048, 1024),
                         wqb[fc * 128:(fc + 1) * 128, :])
        W("g16", ck_wob)
        W("tmm", ck_vmm[NT - 1][1])
        ck_wol = 0
        for fc in range(FCN):
            ck_wol = dma("sync", v16(OFF_R1 + 16384 + fc * 2048, 1024),
                         wob[fc * 128:(fc + 1) * 128, :])

        ck_qmm = [0] * (NTB * FCN)
        ck_eq = [0] * (NTB * FCN)
        ck_eqd = [0] * (NTB * FCN)
        ck_zmm = [0] * (NTB * FCN)
        ck_zev = [0] * NTB
        ck_zdma = [0] * NTB
        for tb in range(NTB):
            for dc in range(FCN):
                k = tb * FCN + dc
                bq = 1 + (k & 1)
                if k >= 2:
                    W("sev", ck_eq[k - 2])
                else:
                    W("sev", ck_kev[NT - 1][1])
                    W("vev", ck_vv[NT - 1][1])
                W("s16", ck_wql)
                ck_qmm[k] = mm_group(
                    pb[bq][:, 0:512],
                    [(v16(OFF_R1 + fc * 2048 + dc * 256, 128),
                      v16(OFF_XNT + fc * 4096 + tb * 1024, 512))
                     for fc in range(FCN)])
                W("tmm", ck_qmm[k])
                if k >= 2:
                    W("s16", ck_eqd[k - 2])
                    W("tmm", ck_zmm[k - 2])
                ck_eq[k] = op("scalar", "sev", lambda s, k=k, bq=bq: s.activation(
                    eqstg[k & 1], pb[bq][:, 0:512], ACT.Exp))
                W("sev", ck_eq[k])
                W("vev", ck_ones)
                if tb >= 1 and dc == 0:
                    W("vev", ck_zev[tb - 1])
                ck_zmm[k] = op("tensor", "tmm", lambda t, k=k, dc=dc: t.matmul(
                    pb[3][0:1, 0:512], ones16, eqstg[k & 1],
                    start=(dc == 0), stop=(dc == FCN - 1)))
                ck_eqd[k] = dma("sync",
                                eqdr[dc * 128:(dc + 1) * 128, tb * TBS:(tb + 1) * TBS],
                                eqstg[k & 1])
            W("tmm", ck_zmm[tb * FCN + FCN - 1])
            if tb >= 1:
                W("s16", ck_zdma[tb - 1])
            ck_zev[tb] = op("vector", "vev", lambda v: v.tensor_copy(
                zstg, pb[3][0:1, 0:512]))
            W("vev", ck_zev[tb])
            ck_zdma[tb] = dma("sync",
                              zrowd[tb * TBS:(tb + 1) * TBS].unsqueeze(0), zstg)

        W("g16", ck_gwb)
        W("s16", ck_car[FCN - 1])
        ck_gwl = dma("sync", gwsb, gwb[0:8, :])
        ck_gwt = [0] * FCN
        for fc in range(FCN):
            pbank = 6 + (fc & 1)
            W("s16", ck_gwl)
            W("vev", ck_cev[FCN - 1])
            if fc >= 2:
                W("vev", ck_gwt[fc - 2])
            tck = op("tensor", "tmm", lambda t, fc=fc, pbank=pbank: t.transpose(
                pb16[pbank][:, 0:8], gwsb[0:8, fc * 128:(fc + 1) * 128], ident[0:8, 0:8]))
            W("tmm", tck)
            ck_gwt[fc] = op("vector", "vev", lambda v, fc=fc, pbank=pbank:
                            v.tensor_copy(gwt16[:, fc * 8:(fc + 1) * 8],
                                          pb16[pbank][:, 0:8]))
        ck_comb = [0] * NT
        ck_combd = [0] * NT
        for tt in range(NT):
            W("vev", ck_sevS)
            for fc in range(FCN):
                W("vev", ck_gwt[fc])
            if tt >= 1:
                W("vev", ck_comb[tt - 1])
            gck = mm_group(
                pb[0][:, 0:8],
                [(v16(OFF_XNT + fc * 4096 + tt * 256, 128),
                  gwt16[:, fc * 8:(fc + 1) * 8]) for fc in range(FCN)])
            W("tmm", gck)
            opq("vector", lambda v: v.tensor_copy(gl, pb[0][:, 0:8]))
            dr("vector")
            opq("vector", lambda v: v.reduce_max(col(C_M1), gl, axis=AX))
            dr("vector")
            opq("vector", lambda v: v.tensor_scalar(t8a, gl, col(C_M1), None, ALU.is_ge))
            dr("vector")
            opq("vector", lambda v: v.scalar_tensor_tensor(t8b, t8a, NEGBIG, gl,
                                                           ALU.mult, ALU.add))
            dr("vector")
            opq("vector", lambda v: v.reduce_max(col(C_M2), t8b, axis=AX))
            dr("vector")
            opq("vector", lambda v: v.tensor_scalar(t8a, gl, col(C_M2), None, ALU.is_ge))
            opq("vector", lambda v: v.tensor_sub(col(C_D21), col(C_M2), col(C_M1)))
            ck_gv = op("vector", "vev", lambda v: v.tensor_scalar_mul(
                col(C_NEGM1), col(C_M1), -1.0))
            W("vev", ck_gv)
            opq("scalar", lambda s: s.activation(col(C_E1), col(C_D21), ACT.Exp))
            dr("scalar")
            ck_ge = op("scalar", "sev", lambda s: s.activation(
                gl, gl, ACT.Exp, bias=col(C_NEGM1)))
            W("sev", ck_ge)
            opq("vector", lambda v: v.tensor_scalar_add(col(C_E1), col(C_E1), 1.0))
            dr("vector")
            opq("vector", lambda v: v.reciprocal(col(C_RDEN), col(C_E1)))
            dr("vector")
            opq("vector", lambda v, tt=tt: v.scalar_tensor_tensor(
                comb[:, tt * 8:(tt + 1) * 8], gl, col(C_RDEN), t8a,
                ALU.mult, ALU.mult))
            dr("vector")
            ck_comb[tt] = op("vector", "vev", lambda v, tt=tt: v.tensor_copy(
                c16stg, comb[:, tt * 8:(tt + 1) * 8]))
            W("vev", ck_comb[tt])
            if tt >= 1:
                W("s16", ck_combd[tt - 1])
            ck_combd[tt] = dma("sync", combd[tt * 128:(tt + 1) * 128, 0:8], c16stg)

        W("s16", ck_zdma[NTB - 1])
        ck_zc = dma("sync", zcol, zrowd.rearrange("(c p) -> p c", p=128),
                    allow_slow_non_contiguous=True)
        W("s16", ck_zc)
        opq("vector", lambda v: v.reciprocal(zcol, zcol))
        dr("vector")
        ck_scol = op("vector", "vev", lambda v: v.tensor_scalar_mul(
            scol, zcol, DSCALE))

        if dbg:
            W("cc", ck_cc)
            dma("sync", dmp["arout"], arout)
            W("s16", ck_eqd[NTB * FCN - 1])
            dma("sync", dmp["eqd"], eqdr)
            W("s16", ck_zdma[NTB - 1])
            dma("sync", dmp["zrow"], zrowd.unsqueeze(0))
            W("vev", ck_comb[NT - 1])
            dma("sync", dmp["comb"], comb)
            W("vev", ck_scol)
            dma("sync", dmp["scol"], scol)
        ck_c16 = [0] * FCN
        for dc in range(FCN):
            W("cc", ck_cc)
            if dc >= 1:
                W("vev", ck_c16[dc - 1])
            W("tmm", ck_qmm[NTB * FCN - 1])
            dma("sync", cstage, arout[dc * 128:(dc + 1) * 128, :])
            ck_cl = dma("sync", col(C_SC),
                        arout[1024:1025, dc * 128:(dc + 1) * 128]
                        .flatten().rearrange("(p c) -> p c", c=1),
                        allow_slow_non_contiguous=True)
            W("s16", ck_cl)
            opq("vector", lambda v: v.reciprocal(col(C_RS), col(C_SC)))
            dr("vector")
            ck_c16[dc] = op("vector", "vev", lambda v, dc=dc: v.tensor_scalar(
                v16(OFF_R1 + dc * 2048, 1024), cstage, col(C_RS), None, ALU.mult))
        ck_c16_all = ck_c16[FCN - 1]

        ck_eqb = [0] * NTB
        ck_amm = [[0] * FCN for _ in range(NTB)]
        ck_aev = [[0] * FCN for _ in range(NTB)]
        ck_omm = [[[0, 0] for _ in range(4)] for _ in range(NTB)]
        ck_ii = [0] * NT
        ck_sq2 = [0] * NT
        for tb in range(NTB):
            if tb >= 1:
                W("tmm", ck_amm[tb - 1][FCN - 1])
            W("vev", ck_comb[NT - 1])
            for dc in range(FCN):
                ck_eqb[tb] = dma("sync", v16(P_EQB + dc * 1024, 512),
                                 eqdr[dc * 128:(dc + 1) * 128, tb * TBS:(tb + 1) * TBS])
            W("vev", ck_c16_all)
            W("s16", ck_eqb[tb])
            W("vev", ck_scol)
            for ec in range(FCN):
                ba = ec & 1
                kk = tb * FCN + ec
                if kk >= 2:
                    W("vev", ck_aev[(kk - 2) // FCN][(kk - 2) % FCN])
                if tb >= 2:
                    W("tmm", ck_omm[tb - 2][3][1])
                ck_amm[tb][ec] = mm_group(
                    pb[ba][:, 0:512],
                    [(v16(OFF_R1 + dc * 2048 + ec * 256, 128),
                      v16(P_EQB + dc * 1024, 512)) for dc in range(FCN)])
                W("tmm", ck_amm[tb][ec])
                ck_aev[tb][ec] = op("vector", "vev", lambda v, tb=tb, ec=ec, ba=ba:
                                    v.tensor_copy(v16(OFF_R2 + (tb & 1) * 8192 + ec * 1024, 512),
                                                  pb[ba][:, 0:512]))
            for q in range(4):
                tt = tb * 4 + q
                for ec in range(FCN):
                    W("vev", ck_aev[tb][ec])
                W("s16", ck_wol)
                W("sev", ck_eq[NTB * FCN - 1])
                W("vev", ck_zev[NTB - 1])
                for dh in range(2):
                    bo_ = 2 + dh
                    if tt >= 1:
                        W("vev", ck_ii[tt - 1])
                    ck_omm[tb][q][dh] = mm_group(
                        pb[bo_][:, 0:512],
                        [(v16(OFF_R2 + (tb & 1) * 8192 + ec * 1024 + q * 256, 128),
                          v16(OFF_R1 + 16384 + ec * 2048 + dh * 1024, 512))
                         for ec in range(FCN)])
                W("tmm", ck_omm[tb][q][1])
                W("g16", ck_rep)
                W("vev", ck_scol)
                if tt >= 1:
                    W("sev", ck_sq2[tt - 1])
                opq("vector", lambda v, tt=tt: v.scalar_tensor_tensor(
                    ztile[:, 0:512], pb[2][:, 0:512], scol[:, tt:tt + 1],
                    borep[:, 0:512], ALU.mult, ALU.add))
                opq("vector", lambda v, tt=tt: v.scalar_tensor_tensor(
                    ztile[:, 512:1024], pb[3][:, 0:512], scol[:, tt:tt + 1],
                    borep[:, 512:1024], ALU.mult, ALU.add))
                opq("vector", lambda v, tt=tt: v.tensor_tensor(
                    ztile, ztile, v16(OFF_XN16 + tt * 2048, 1024), ALU.add))
                dr("vector")
                ck_z1 = op("vector", "vev", lambda v: v.reduce_sum(
                    col(C_SUM), ztile, axis=AX))
                W("vev", ck_z1)
                ck_sq2[tt] = op("scalar", "sev", lambda s: s.activation(
                    sqscr2, ztile, ACT.Square))
                W("sev", ck_sq2[tt])
                opq("vector", lambda v: v.reduce_sum(col(C_SQ), sqscr2, axis=AX))
                dr("vector")
                opq("vector", lambda v: v.tensor_scalar_mul(col(C_MEAN), col(C_SUM), 1.0 / D))
                opq("vector", lambda v: v.tensor_scalar(col(C_T0), col(C_SQ), 1.0 / D,
                                                        EPS, ALU.mult, ALU.add))
                dr("vector")
                opq("vector", lambda v: v.tensor_tensor(col(C_T1), col(C_MEAN), col(C_MEAN), ALU.mult))
                dr("vector")
                ck_v2 = op("vector", "vev", lambda v: v.tensor_sub(
                    col(C_VAR), col(C_T0), col(C_T1)))
                W("vev", ck_v2)
                opq("scalar", lambda s: s.activation(col(C_T0), col(C_VAR), ACT.Ln))
                dr("scalar")
                ck_r2 = op("scalar", "sev", lambda s: s.activation(
                    col(C_RSTD), col(C_T0), ACT.Exp, scale=-0.5))
                W("sev", ck_r2)
                opq("vector", lambda v: v.scalar_tensor_tensor(
                    col(C_NMR), col(C_MEAN), -1.0, col(C_RSTD), ALU.mult, ALU.mult))
                dr("vector")
                opq("vector", lambda v: v.tensor_scalar(
                    ztile, ztile, col(C_RSTD), col(C_NMR), ALU.mult, ALU.add))
                dr("vector")
                opq("vector", lambda v: v.tensor_tensor(ztile, ztile, gamrep, ALU.mult))
                dr("vector")
                ck_ii[tt] = op("vector", "vev", lambda v, tt=tt: v.tensor_tensor(
                    v32(OFF_EKV + tt * 4096, 1024), ztile, betrep, ALU.add))
        ck_attn2_done = ck_ii[NT - 1]
        ck_ao_mm_last = ck_omm[NTB - 1][3][1]
        ck_sq2_last = ck_sq2[NT - 1]

        if dbg:
            W("vev", ck_c16_all)
            dma("sync", dmp["c16"], v16(OFF_R1, 8192))
            W("vev", ck_attn2_done)
            dma("sync", dmp["acc1"], v32(OFF_EKV, 16384))
        ck_w1t = [0] * E
        ck_hmm = {}
        ck_hev = {}
        ck_eomm = {}
        ck_eov = {}
        for e in range(E):
            W("g16", ck_w1c[e])
            if e == 0:
                W("tmm", ck_amm[NTB - 1][FCN - 1])
                W("tmm", ck_ao_mm_last)
                W("vev", ck_attn2_done)
                W("sev", ck_sq2_last)
            else:
                W("tmm", ck_hmm[(e - 1, NTB - 1, HCN - 1)])
            for fc in range(FCN):
                dst = (v16(OFF_R1 + fc * 8192, 4096) if fc < 4
                       else v16(OFF_R2 + (fc - 4) * 8192, 4096))
                ck_w1t[e] = dmat("sync", dst,
                                 w1br[e * H:(e + 1) * H, fc * 128:(fc + 1) * 128])
            if dbg and e >= 1:
                W("vev", ck_eov[(e - 1, NTB - 1, 7)])
                ck_edump = dma("sync", dmp[f"acce{e-1}"], v32(OFF_EKV, 16384))
            for tb in range(NTB):
                for hc in range(HCN):
                    bh = hc & 1
                    if dbg and e >= 1 and tb == 0 and hc == 0:
                        W("s16", ck_edump)
                    W("s16", ck_w1t[e])
                    if hc >= 2:
                        W("sev", ck_hev[(e, tb, hc - 2)])
                    elif (e, tb) != (0, 0):
                        pe_, ptb_ = (e, tb - 1) if tb >= 1 else (e - 1, NTB - 1)
                        W("vev", ck_eov[(pe_, ptb_, bh)])
                    else:
                        W("vev", ck_attn2_done)
                    ck_hmm[(e, tb, hc)] = mm_group(
                        pb[bh][:, 0:512],
                        [((v16(OFF_R1 + fc * 8192 + hc * 256, 128) if fc < 4
                           else v16(OFF_R2 + (fc - 4) * 8192 + hc * 256, 128)),
                          v16(OFF_XNT + fc * 4096 + tb * 1024, 512))
                         for fc in range(FCN)])
                    W("tmm", ck_hmm[(e, tb, hc)])
                    W("s16", ck_b1)
                    if (e, tb) == (0, 0):
                        W("vev", ck_attn2_done)
                    else:
                        pe_, ptb_ = (e, tb - 1) if tb >= 1 else (e - 1, NTB - 1)
                        W("tmm", ck_eomm[(pe_, ptb_, HCN - 1)])
                    ck_hev[(e, tb, hc)] = op("scalar", "sev", lambda s, e=e, hc=hc, bh=bh:
                                             s.activation(v16(OFF_XN16 + hc * 1024, 512),
                                                          pb[bh][:, 0:512], ACT.Relu,
                                                          bias=b1all[:, e * 32 + hc: e * 32 + hc + 1]))
                if dbg and (e, tb) == (0, 0):
                    W("sev", ck_hev[(0, 0, HCN - 1)])
                    ck_h0d = dma("sync", dmp["h0"], v16(OFF_XN16, 16384))
                    W("s16", ck_h0d)
                W("sev", ck_hev[(e, tb, HCN - 1)])
                for g in range(8):
                    if (e, tb) != (0, 0):
                        pe_, ptb_ = (e, tb - 1) if tb >= 1 else (e - 1, NTB - 1)
                        W("vev", ck_eov[(pe_, ptb_, g)])
                    else:
                        W("vev", ck_attn2_done)
                for hc in range(HCN):
                    klin = (e * NTB + tb) * HCN + hc
                    slot = hc & 1
                    if klin >= 2:
                        k2 = klin - 2
                        pk = (k2 // (NTB * HCN), (k2 // HCN) % NTB, k2 % HCN)
                        W("tmm", ck_eomm[pk])
                    W("g16", ck_w2c[e])
                    if (e, tb, hc) == (0, 0, 0):
                        W("vev", ck_attn2_done)
                    ck_w2s = dmat("sync", v16(P_W2S[slot], 1024),
                                  w2br[e * D:(e + 1) * D, hc * 128:(hc + 1) * 128])
                    W("s16", ck_w2s)
                    for g in range(8):
                        ts_, dh_ = g >> 1, g & 1
                        st, sp = (hc == 0), (hc == HCN - 1)
                        if g == 7:
                            ck_eomm[(e, tb, hc)] = op("tensor", "tmm",
                                lambda t, hc=hc, ts_=ts_, dh_=dh_, g=g, slot=slot, st=st, sp=sp:
                                t.matmul(pb[g][:, 0:512],
                                         v16(OFF_XN16 + hc * 1024 + ts_ * 256, 128),
                                         v16(P_W2S[slot] + dh_ * 1024, 512),
                                         start=st, stop=sp))
                        else:
                            opq("tensor",
                                lambda t, hc=hc, ts_=ts_, dh_=dh_, g=g, slot=slot, st=st, sp=sp:
                                t.matmul(pb[g][:, 0:512],
                                         v16(OFF_XN16 + hc * 1024 + ts_ * 256, 128),
                                         v16(P_W2S[slot] + dh_ * 1024, 512),
                                         start=st, stop=sp))
                W("tmm", ck_eomm[(e, tb, HCN - 1)])
                for g in range(8):
                    ts_, dh_ = g >> 1, g & 1
                    tt = tb * 4 + ts_
                    ck_eov[(e, tb, g)] = op("vector", "vev",
                        lambda v, e=e, tt=tt, dh_=dh_, g=g: v.scalar_tensor_tensor(
                            v32(OFF_EKV + tt * 4096 + dh_ * 2048, 512),
                            pb[g][:, 0:512],
                            comb[:, tt * 8 + e: tt * 8 + e + 1],
                            v32(OFF_EKV + tt * 4096 + dh_ * 2048, 512),
                            ALU.mult, ALU.add))

        ck_moe_done = ck_eov[(E - 1, NTB - 1, 7)]

        if dbg:
            W("vev", ck_eov[(E - 1, NTB - 1, 7)])
            ck_edump2 = dma("sync", dmp[f"acce{E-1}"], v32(OFF_EKV, 16384))
            W("s16", ck_edump2)
        W("tmm", ck_hmm[(E - 1, NTB - 1, HCN - 1)])
        W("s16", ck_combd[NT - 1])
        ck_combT = dmat("sync", combT, combd)
        W("tmm", ck_amm[NTB - 1][FCN - 1])
        ck_b216 = dma("gpsimd", b216, b2)
        ck_b2mm = [0] * (NT * 2)
        ck_b2ev = [0] * (NT * 2)
        for tt in range(NT):
            for dh in range(2):
                kk = tt * 2 + dh
                bb = 2 + (kk & 1)
                W("s16", ck_combT)
                W("g16", ck_b216)
                W("vev", ck_moe_done)
                if kk >= 2:
                    W("vev", ck_b2ev[kk - 2])
                ck_b2mm[kk] = op("tensor", "tmm", lambda t, tt=tt, dh=dh, bb=bb:
                                 t.matmul(pb[bb][:, 0:512],
                                          combT[0:8, tt * 128:(tt + 1) * 128],
                                          b216[0:8, dh * 512:(dh + 1) * 512],
                                          start=True, stop=True))
                W("tmm", ck_b2mm[kk])
                ck_b2ev[kk] = op("vector", "vev", lambda v, tt=tt, dh=dh, bb=bb:
                                 v.tensor_tensor(v32(OFF_EKV + tt * 4096 + dh * 2048, 512),
                                                 v32(OFF_EKV + tt * 4096 + dh * 2048, 512),
                                                 pb[bb][:, 0:512], ALU.add))

        if dbg:
            W("vev", ck_moe_done)
            W("vev", ck_b2ev[NT * 2 - 1])
            dma("sync", dmp["acc2"], v32(OFF_EKV, 16384))
        ck_add = [0] * NT
        ck_y = [0] * NT
        for tt in range(NT):
            if tt >= 1:
                W("vev", ck_add[tt - 1])
            ck_xf = dma("sync", xafin, xg[tt * 128:(tt + 1) * 128, :])
            W("s16", ck_xf)
            W("vev", ck_b2ev[tt * 2 + 1])
            if tt >= 1:
                W("s16", ck_y[tt - 1])
            ck_add[tt] = op("vector", "vev", lambda v, tt=tt: v.tensor_tensor(
                ostg, v32(OFF_EKV + tt * 4096, 1024), xafin, ALU.add))
            W("vev", ck_add[tt])
            ck_y[tt] = dma("sync", yg[tt * 128:(tt + 1) * 128, :], ostg)
        W("s16", ck_y[NT - 1])

    with nc.Block() as block:
        @block.sync
        def _(e):
            emit("sync", e)

        @block.gpsimd
        def _(e):
            emit("gpsimd", e)

        @block.tensor
        def _(e):
            emit("tensor", e)

        @block.vector
        def _(e):
            emit("vector", e)

        @block.scalar
        def _(e):
            emit("scalar", e)

    return nc


_NC_CACHE = None


def kernel(L0_attn_x, L0_attn_Wq, L0_attn_Wk, L0_attn_Wv, L0_attn_Wo,
           L0_attn_bo, L0_attn_gamma, L0_attn_beta, L0_ffn_gate_w,
           L0_ffn_w1, L0_ffn_b1, L0_ffn_w2, L0_ffn_b2):
    global _NC_CACHE
    x = np.ascontiguousarray(np.asarray(L0_attn_x, dtype=np.float32)).reshape(T, D)
    wts = {
        "wq": np.ascontiguousarray(np.asarray(L0_attn_Wq, dtype=np.float32)),
        "wk": np.ascontiguousarray(np.asarray(L0_attn_Wk, dtype=np.float32)),
        "wv": np.ascontiguousarray(np.asarray(L0_attn_Wv, dtype=np.float32)),
        "wo": np.ascontiguousarray(np.asarray(L0_attn_Wo, dtype=np.float32)),
        "bo": np.ascontiguousarray(np.asarray(L0_attn_bo, dtype=np.float32)),
        "gamma": np.ascontiguousarray(np.asarray(L0_attn_gamma, dtype=np.float32)),
        "beta": np.ascontiguousarray(np.asarray(L0_attn_beta, dtype=np.float32)),
        "gate_w": np.ascontiguousarray(np.asarray(L0_ffn_gate_w, dtype=np.float32)),
        "w1": np.ascontiguousarray(np.asarray(L0_ffn_w1, dtype=np.float32)),
        "b1": np.ascontiguousarray(np.asarray(L0_ffn_b1, dtype=np.float32)),
        "w2": np.ascontiguousarray(np.asarray(L0_ffn_w2, dtype=np.float32)),
        "b2": np.ascontiguousarray(np.asarray(L0_ffn_b2, dtype=np.float32)),
    }
    if _NC_CACHE is None:
        _NC_CACHE = build_nc()
    in_maps = []
    for c in range(NCORES):
        m = {"x": np.ascontiguousarray(x[c * TC:(c + 1) * TC, :])}
        m.update(wts)
        in_maps.append(m)
    res = run_bass_kernel_spmd(_NC_CACHE, in_maps, core_ids=list(range(NCORES)))
    y = np.concatenate([res.results[c]["y"] for c in range(NCORES)], axis=0)
    return y.reshape(1, T, D)
